# version 12 (final)
# baseline (speedup 1.0000x reference)
import numpy as np
import concourse.bass as bass
import concourse.mybir as mybir
from concourse.bass_utils import run_bass_kernel_spmd

F32 = mybir.dt.float32
F32R = mybir.dt.float32r
I32 = mybir.dt.int32
U32 = mybir.dt.uint32
AF = mybir.ActivationFunctionType
ALU = mybir.AluOpType
AX = mybir.AxisListType

NCORES = 8
D = 2048
T = 1024
TH = 1027
TH_P = 1040
S_LEN = 8192
DIN = 4096
NH = 64
HD = 64
CHUNK = 256
NCH = 4
CONV_DIM = 4352
DPROJ = 8512
E = 16
CAP = 1024
I_FF = 1408
EPS = 1e-5

ENGINES = ("sync", "tensor", "vector", "scalar", "gpsimd")


class Rec:
    def __init__(self, sem_names):
        self.ops = {e: [] for e in ENGINES}
        self.cnt = {s: 0 for s in sem_names}
        self.sem = None

    AUTOSEM = {"vector": "vec", "scalar": "sca"}

    def op(self, eng, meth, *args, inc=None, incv=None, **kwargs):
        auto = self.AUTOSEM.get(eng)
        if auto is not None and "dma" not in meth:
            if self.cnt[auto] > 0:
                self.ops[eng].append(("wait", (auto, self.cnt[auto])))
            if inc is None:
                inc = auto
        if inc is not None and incv is None:
            incv = 16 if ("dma" in meth) else 1
        if inc is not None:
            self.cnt[inc] += incv
        self.ops[eng].append(("op", (meth, args, kwargs, inc, incv)))
        return self.cnt[inc] if inc else None

    def wait(self, eng, sem, val=None):
        if val is None:
            val = self.cnt[sem]
        if val > 0:
            self.ops[eng].append(("wait", (sem, val)))

    def replay(self, eng_obj, eng_name):
        reg_map = {}

        def _resolve(v):
            if isinstance(v, tuple) and len(v) == 2 and v[0] == "REG":
                if v[1] not in reg_map:
                    reg = eng_obj.alloc_register()
                    eng_obj.reg_mov(reg, v[1])
                    reg_map[v[1]] = reg
                return reg_map[v[1]]
            return v

        for kind, rec in self.ops[eng_name]:
            if kind == "wait":
                sem, val = rec
                eng_obj.wait_ge(self.sem[sem], val)
            else:
                meth, args, kwargs, inc, incv = rec
                kwargs = {k: _resolve(v) for k, v in kwargs.items()}
                res = getattr(eng_obj, meth)(*args, **kwargs)
                if inc is not None:
                    assert res is not None, f"{meth} returned None"
                    res.then_inc(self.sem[inc], incv)


PHASES = 99
DEBUG = False
LAST_RES = None


def build_program(phases=None):
    if phases is None:
        phases = PHASES
    import contextlib
    nc = bass.Bass("TRN2", target_bir_lowering=False)

    def din(name, shape, dtype=F32):
        return nc.dram_tensor(name, list(shape), dtype, kind="ExternalInput").ap()

    dbg_names = {"zx_d", "cumA_d", "s_d", "sh_d"}

    def dint(name, shape, dtype=F32):
        kind = "ExternalOutput" if (DEBUG and name in dbg_names) else "Internal"
        return nc.dram_tensor(name, list(shape), dtype, kind=kind).ap()

    u = din("u", (TH, D))
    w_in = din("w_in", (D, DPROJ))
    w_out = din("w_out", (DIN, D))
    conv_wT = din("conv_wT", (128, 136))
    conv_b = din("conv_b", (128, 34))
    dt_bias = din("dt_bias", (NH, 1))
    a_log = din("a_log", (NH, 1))
    d_row = din("d_row", (1, NH))
    ns_row = din("ns_row", (1, DIN))
    gate_wT = din("gate_wT", (D, E))
    gate_b = din("gate_b", (1, E))
    ws1T = din("ws1T", (D, I_FF))
    ws3T = din("ws3T", (D, I_FF))
    ws2T = din("ws2T", (I_FF, D))
    w1T = din("w1T", (2, D, I_FF))
    w3T = din("w3T", (2, D, I_FF))
    w2T = din("w2T", (2, I_FF, D))
    ident_d = din("ident", (128, 128))
    triu_strict = din("triu_strict", (128, 128))
    triu_inc = din("triu_inc", (128, 128))
    allones_d = din("allones", (128, 128))
    gmask0_d = din("gmask0", (128, 256))
    gmask1_d = din("gmask1", (128, 256))
    iota16_d = din("iota16", (128, 16))
    premask_d = din("premask", (8, 1))
    wmaskmat_d = din("wmaskmat", (8, 8))
    wmaskbias_d = din("wmaskbias", (64, 8))
    tokid1_d = din("tokid1", (128, 8))
    c99999_d = din("c99999", (128, 1))
    ceps_d = din("ceps", (128, 1))

    out = nc.dram_tensor("out", [T, D], F32, kind="ExternalOutput").ap()

    xn_tm = dint("xn_tm", (T, D))
    xnt_hi_d = dint("xnt_hi_d", (D, TH_P), F32R)
    xnt_lo_d = dint("xnt_lo_d", (D, TH_P), F32R)
    zx_d = dint("zx_d", (DPROJ, TH))
    cumA_d = dint("cumA_d", (NH, T))
    s_d = dint("s_d", (T, D))
    sh_d = dint("sh_d", (T, D))
    out_tab = dint("out_tab", (2 * CAP, D))
    glist_in = dint("glist_in", (E * CAP, 1))
    cnt_in = dint("cnt_in", (1, E))
    alast_in = dint("alast_in", (1, NH))
    stat_in = dint("stat_in", (128, NH * HD))
    xn_all = dint("xn_all", (S_LEN, D))
    cnt_all = dint("cnt_all", (NCORES, E))
    glist_own = dint("glist_own", (2 * CAP, 1))
    stat_all = dint("stat_all", (128 * NCORES, NH * HD))
    alast_all = dint("alast_all", (NCORES, NH))
    outs_all = dint("outs_all", (E * CAP, D))

    RG = [[0, 1, 2, 3, 4, 5, 6, 7]]
    sems = ["ld", "mm", "vec", "sca", "gp", "coll", "spill", "st"]
    r = Rec(sems)
    stack = contextlib.ExitStack()

    def sbuf(name, shape, dtype=F32):
        return stack.enter_context(nc.sbuf_tensor(name, list(shape), dtype))

    CL1 = sbuf("CL1", (128, 6912))
    CL2 = sbuf("CL2", (128, 6912))
    WH = sbuf("WH", (128, 2048))
    WL = sbuf("WL", (128, 2048))
    GW = sbuf("GW", (128, 512))
    D1 = sbuf("D1", (128, 8192))
    D2 = sbuf("D2", (128, 8192))
    RC = sbuf("RC", (128, 4096))
    RD = sbuf("RD", (128, 4096))
    RE = sbuf("RE", (128, 1024))
    WS = sbuf("WS", (128, 2048))
    CHIN = sbuf("CHIN", (128, 2 * 259))
    CONVO = sbuf("CONVO", (128, 2 * 256))
    BCB = sbuf("BCB", (128, 768))
    MISC = sbuf("MISC", (128, 1536))
    CST = sbuf("CST", (128, 1344))
    SMALL = sbuf("SMALL", (128, 1024))
    MT = sbuf("MT", (128, 512))
    GT = sbuf("GT", (128, 512))
    PSB = [stack.enter_context(nc.psum_tensor(f"p{i}", [128, 512], F32)) for i in range(8)]
    sem_objs = {s: stack.enter_context(nc.semaphore(s)) for s in sems}
    r.sem = sem_objs

    C_ID, C_TS, C_TI, C_ON = 0, 128, 256, 384
    C_G0, C_G1 = 512, 768
    C_IO, C_GB, C_99, C_EPS = 1024, 1040, 1056, 1057
    C_DTB, C_ALOG, C_A = 1058, 1059, 1060
    C_DCOL = 1061
    C_CW, C_CB = 1125, 1261
    C_PM = 1295
    C_TOK = 1296
    C_WMM = 1304
    C_WMB = 1312
    M_OH1, M_OH2, M_OHS, M_SCO = 0, 128, 256, 384
    M_P, M_PG = 512, 640
    M_CNT, M_CNTA, M_BASE = 768, 784, 800
    M_Q = 816
    M_CUMA, M_DT, M_DEC, M_EXPC = 848, 976, 1104, 1232
    M_AL = 1360

    def mark(eng="vector"):
        r.op(eng, "tensor_copy", SMALL[0:1, 1023:1024], SMALL[0:1, 1022:1023],
             inc="vec" if eng == "vector" else "sca")

    def mm(out_ap, lhsT, rhs, start, stop, inc=None):
        return r.op("tensor", "matmul", out_ap, lhsT, rhs, start=start, stop=stop, inc=inc)

    def tpose(out_ap, in_ap, inc=None):
        npart = in_ap.shape[0]
        return r.op("tensor", "transpose", out_ap, in_ap,
                    CST[0:npart, C_ID:C_ID + npart], inc=inc)

    def vop(meth, *a, inc=None, **k):
        return r.op("vector", meth, *a, inc=inc, **k)

    def act(out_ap, in_ap, fn, bias=None, scale=None, inc=None):
        kw = {}
        if bias is not None:
            kw["bias"] = bias
        if scale is not None:
            kw["scale"] = scale
        return r.op("scalar", "activation", out_ap, in_ap, fn, inc=inc, **kw)

    def dma(eng, out_ap, in_ap, inc="ld"):
        return r.op(eng, "dma_start", out_ap, in_ap, inc=inc, incv=16)

    def hilo(hi_ap, lo_ap, src_ap):
        r.op("vector", "tensor_copy", hi_ap.bitcast(F32R), src_ap)
        r.op("vector", "tensor_tensor", lo_ap.bitcast(F32R), src_ap, hi_ap, ALU.subtract)

    def fm_ap(dsrc, npart, nk, ncols, row0, col0, row_stride):
        return bass.AP(dsrc.tensor, dsrc.offset + row0 * row_stride + col0,
                       [[row_stride, npart], [row_stride * 128, nk], [1, ncols]])

    dma("sync", CST[:, C_ID:C_ID + 128], ident_d)
    dma("sync", CST[:, C_TS:C_TS + 128], triu_strict)
    dma("sync", CST[:, C_TI:C_TI + 128], triu_inc)
    dma("sync", CST[:, C_ON:C_ON + 128], allones_d)
    dma("sync", CST[:, C_G0:C_G0 + 256], gmask0_d)
    dma("sync", CST[:, C_G1:C_G1 + 256], gmask1_d)
    dma("sync", CST[:, C_IO:C_IO + 16], iota16_d)
    dma("sync", CST[:, C_GB:C_GB + 16], gate_b.partition_broadcast(128).squeeze(1))
    dma("sync", CST[:, C_99:C_99 + 1], c99999_d)
    dma("sync", CST[:, C_EPS:C_EPS + 1], ceps_d)
    dma("sync", CST[0:64, C_DTB:C_DTB + 1], dt_bias)
    dma("sync", CST[0:64, C_ALOG:C_ALOG + 1], a_log)
    dma("sync", CST[:, C_DCOL:C_DCOL + 64], d_row.partition_broadcast(128).squeeze(1))
    dma("sync", CST[:, C_CW:C_CW + 136], conv_wT)
    dma("sync", CST[:, C_CB:C_CB + 34], conv_b)
    dma("sync", CST[0:8, C_PM:C_PM + 1], premask_d)
    dma("sync", CST[:, C_TOK:C_TOK + 8], tokid1_d)
    dma("sync", CST[0:8, C_WMM:C_WMM + 8], wmaskmat_d)
    dma("sync", CST[0:64, C_WMB:C_WMB + 8], wmaskbias_d)
    dma("sync", SMALL[:, 0:256], fm_ap(gate_wT, 128, 16, 16, 0, 0, E))
    LD_CONST = r.cnt["ld"]
    r.wait("scalar", "ld", LD_CONST)
    act(SMALL[0:64, 256:257], CST[0:64, C_ALOG:C_ALOG + 1], AF.Exp, inc="sca")
    r.wait("vector", "sca")
    vop("tensor_scalar", CST[0:64, C_A:C_A + 1], SMALL[0:64, 256:257], -1.0, None, ALU.mult)
    r.wait("vector", "ld", LD_CONST)
    hilo(GW[:, 0:256], GW[:, 256:512], SMALL[:, 0:256])
    vop("memset", SMALL[:, 0:256], 0.0, inc="vec")
    r.wait("sync", "vec")
    dma("sync", bass.AP(glist_in.tensor, 0, [[128, 128], [1, 128]]), SMALL[:, 0:128],
        inc="spill")
    dma("sync", bass.AP(xnt_hi_d.tensor, 1027, [[TH_P, 128], [TH_P * 128, 16], [1, 13]]),
        bass.AP(SMALL, 0, [[1024, 128], [13, 16], [1, 13]]).bitcast(F32R), inc="spill")
    dma("sync", bass.AP(xnt_lo_d.tensor, 1027, [[TH_P, 128], [TH_P * 128, 16], [1, 13]]),
        bass.AP(SMALL, 0, [[1024, 128], [13, 16], [1, 13]]).bitcast(F32R), inc="spill")

    for tt in range(9):
        rows = 128 if tt < 8 else 3
        r.wait("sync", "mm")
        ld_tile = dma("sync", D1[0:rows, 0:2048], u[tt * 128: tt * 128 + rows, :])
        r.wait("vector", "ld", ld_tile)
        r.wait("vector", "mm")
        r.wait("vector", "spill")
        xt = D1[0:rows, 0:2048]
        xc = D1[0:rows, 2048:4096]
        vop("reduce_sum", SMALL[0:rows, 0:1], xt, AX.X, inc="vec")
        r.wait("vector", "vec")
        vop("tensor_scalar", SMALL[0:rows, 0:1], SMALL[0:rows, 0:1], 1.0 / D, None, ALU.mult,
            inc="vec")
        r.wait("vector", "vec")
        vop("tensor_scalar", xc, xt, SMALL[0:rows, 0:1], None, ALU.subtract, inc="vec")
        r.wait("scalar", "vec")
        act(RE[0:rows, 0:1024], xc[0:rows, 0:1024], AF.Square, inc="sca")
        r.wait("vector", "sca")
        vop("reduce_sum", SMALL[0:rows, 1:2], RE[0:rows, 0:1024], AX.X, inc="vec")
        r.wait("scalar", "vec")
        act(RE[0:rows, 0:1024], xc[0:rows, 1024:2048], AF.Square, inc="sca")
        r.wait("vector", "sca")
        vop("reduce_sum", SMALL[0:rows, 3:4], RE[0:rows, 0:1024], AX.X, inc="vec")
        r.wait("vector", "vec")
        vop("tensor_tensor", SMALL[0:rows, 1:2], SMALL[0:rows, 1:2], SMALL[0:rows, 3:4],
            ALU.add, inc="vec")
        r.wait("scalar", "vec")
        act(SMALL[0:rows, 2:3], SMALL[0:rows, 1:2], AF.Sqrt,
            bias=CST[0:rows, C_EPS:C_EPS + 1], scale=1.0 / D, inc="sca")
        r.wait("vector", "sca")
        vop("reciprocal", SMALL[0:rows, 3:4], SMALL[0:rows, 2:3], inc="vec")
        r.wait("vector", "vec")
        vop("tensor_scalar", xc, xc, SMALL[0:rows, 3:4], None, ALU.mult, inc="vec")
        if tt < 8:
            r.wait("sync", "vec")
            dma("sync", xn_tm[tt * 128:(tt + 1) * 128, :], xc, inc="spill")
        r.wait("tensor", "vec")
        r.wait("vector", "spill")
        for k in range(16):
            pb = PSB[2 + (k % 2)]
            if k >= 2:
                r.wait("tensor", "vec")
            tpose(pb[0:128, 0:rows], xc[0:rows, k * 128:(k + 1) * 128], inc="mm")
            r.wait("vector", "mm")
            hilo(CL1[:, k * 128:k * 128 + rows], CL2[:, k * 128:k * 128 + rows],
                 pb[0:128, 0:rows])
            mark()
        r.wait("sync", "vec")
        col0 = tt * 128
        dma("sync", fm_ap(xnt_hi_d, 128, 16, rows, 0, col0, TH_P),
            bass.AP(CL1, 0, [[6912, 128], [128, 16], [1, rows]]).bitcast(F32R), inc="spill")
        dma("sync", fm_ap(xnt_lo_d, 128, 16, rows, 0, col0, TH_P),
            bass.AP(CL2, 0, [[6912, 128], [128, 16], [1, rows]]).bitcast(F32R), inc="spill")
    SPILL_XN = r.cnt["spill"]

    def finish():
        r.wait("vector", "spill")
        vop("memset", D1[:, 0:2048], 0.0, inc="vec")
        r.wait("sync", "vec")
        for tt in range(8):
            dma("sync", out[tt * 128:(tt + 1) * 128, :], D1[:, 0:2048], inc="st")
        r.wait("sync", "st")
        with nc.Block() as block:
            @block.sync
            def _(sync):
                r.replay(sync, "sync")

            @block.tensor
            def _(tensor):
                r.replay(tensor, "tensor")

            @block.vector
            def _(vector):
                r.replay(vector, "vector")

            @block.scalar
            def _(scalar):
                r.replay(scalar, "scalar")

            @block.gpsimd
            def _(gpsimd):
                r.replay(gpsimd, "gpsimd")
        stack.close()
        return nc

    if phases < 2:
        return finish()
    r.wait("gpsimd", "spill", SPILL_XN)
    r.op("gpsimd", "collective_compute", "AllGather", ALU.bypass, RG,
         [xn_tm], [xn_all], inc="coll")

    if phases < 3:
        return finish()
    for tt in range(8):
        r.wait("sync", "mm")
        r.wait("sync", "spill")
        lh = dma("sync", CL1[:, 0:2048].bitcast(F32R),
                 fm_ap(xnt_hi_d, 128, 16, 128, 0, tt * 128, TH_P))
        ll = dma("sync", CL2[:, 0:2048].bitcast(F32R),
                 fm_ap(xnt_lo_d, 128, 16, 128, 0, tt * 128, TH_P))
        r.wait("tensor", "ld", ll)
        r.wait("tensor", "vec")
        r.wait("tensor", "sca")
        pb = PSB[0]
        for k in range(16):
            mm(pb[:, 0:16], CL1[:, k * 128:(k + 1) * 128].bitcast(F32R),
               GW[:, 0:256][:, k * 16:(k + 1) * 16].bitcast(F32R),
               start=(k == 0), stop=False)
            mm(pb[:, 0:16], CL1[:, k * 128:(k + 1) * 128].bitcast(F32R),
               GW[:, 256:512][:, k * 16:(k + 1) * 16].bitcast(F32R),
               start=False, stop=False)
            mm(pb[:, 0:16], CL2[:, k * 128:(k + 1) * 128].bitcast(F32R),
               GW[:, 0:256][:, k * 16:(k + 1) * 16].bitcast(F32R),
               start=False, stop=(k == 15), inc="mm" if k == 15 else None)
        r.wait("scalar", "mm")
        act(SMALL[:, 0:16], pb[:, 0:16], AF.Exp, inc="sca")
        r.op("scalar", "add", SMALL[:, 0:16], SMALL[:, 0:16], 1.0)
        act(SMALL[:, 0:16], SMALL[:, 0:16], AF.Ln, inc="sca")
        r.wait("scalar", "sca")
        act(MISC[:, M_SCO + tt * 16:M_SCO + (tt + 1) * 16], SMALL[:, 0:16], AF.Sqrt, inc="sca")
        r.wait("vector", "sca")
        sco = MISC[:, M_SCO + tt * 16:M_SCO + (tt + 1) * 16]
        vop("tensor_tensor", SMALL[:, 16:32], sco, CST[:, C_GB:C_GB + 16], ALU.add,
            inc="vec")
        r.wait("vector", "vec")
        vop("max", SMALL[:, 32:40], SMALL[:, 16:32], inc="vec")
        r.wait("vector", "vec")
        vop("max_index", SMALL[:, 40:48].bitcast(U32), SMALL[:, 32:40], SMALL[:, 16:32],
            inc="vec")
        r.wait("vector", "vec")
        vop("tensor_copy", SMALL[:, 48:49], SMALL[:, 40:41].bitcast(U32))
        vop("tensor_copy", SMALL[:, 49:50], SMALL[:, 41:42].bitcast(U32), inc="vec")
        r.wait("vector", "vec")
        oh1 = MISC[:, M_OH1 + tt * 16:M_OH1 + (tt + 1) * 16]
        oh2 = MISC[:, M_OH2 + tt * 16:M_OH2 + (tt + 1) * 16]
        vop("tensor_scalar", oh1, CST[:, C_IO:C_IO + 16], SMALL[:, 48:49], None, ALU.is_equal)
        vop("tensor_scalar", oh2, CST[:, C_IO:C_IO + 16], SMALL[:, 49:50], None, ALU.is_equal)
        vop("tensor_tensor", MISC[:, M_OHS + tt * 16:M_OHS + (tt + 1) * 16], oh1, oh2, ALU.add)
        vop("tensor_tensor", SMALL[:, 64:80], sco, oh1, ALU.mult, inc="vec")
        r.wait("vector", "vec")
        vop("reduce_sum", SMALL[:, 50:51], SMALL[:, 64:80], AX.X, inc="vec")
        r.wait("vector", "vec")
        vop("tensor_tensor", SMALL[:, 64:80], sco, oh2, ALU.mult, inc="vec")
        r.wait("vector", "vec")
        vop("reduce_sum", SMALL[:, 51:52], SMALL[:, 64:80], AX.X, inc="vec")
        r.wait("vector", "vec")
        vop("tensor_tensor", SMALL[:, 52:53], SMALL[:, 50:51], SMALL[:, 51:52], ALU.add)
        qb = M_Q + tt * 4
        vop("reciprocal", SMALL[:, 53:54], SMALL[:, 52:53], inc="vec")
        r.wait("vector", "vec")
        vop("tensor_tensor", MISC[:, qb + 2:qb + 3], SMALL[:, 50:51], SMALL[:, 53:54], ALU.mult)
        vop("tensor_tensor", MISC[:, qb + 3:qb + 4], SMALL[:, 51:52], SMALL[:, 53:54], ALU.mult)
        vop("tensor_copy", MISC[:, qb:qb + 1], SMALL[:, 48:49])
        vop("tensor_copy", MISC[:, qb + 1:qb + 2], SMALL[:, 49:50], inc="vec")

    for tt in range(8):
        pb = PSB[6]
        r.wait("tensor", "vec")
        for k in range(tt + 1):
            lhsT = CST[:, C_TS:C_TS + 128] if k == tt else CST[:, C_ON:C_ON + 128]
            mm(pb[:, 0:16], lhsT, MISC[:, M_OHS + k * 16:M_OHS + (k + 1) * 16],
               start=(k == 0), stop=(k == tt), inc="mm" if k == tt else None)
        r.wait("vector", "mm")
        vop("tensor_copy", MISC[:, M_P + tt * 16:M_P + (tt + 1) * 16], pb[:, 0:16], inc="vec")
    r.wait("tensor", "vec")
    pbt = PSB[5]
    for k in range(8):
        mm(pbt[0:1, 0:16], CST[:, C_ON:C_ON + 1], MISC[:, M_OHS + k * 16:M_OHS + (k + 1) * 16],
           start=(k == 0), stop=(k == 7), inc="mm" if k == 7 else None)
    r.wait("vector", "mm")
    vop("tensor_copy", MISC[0:1, M_CNT:M_CNT + 16], pbt[0:1, 0:16], inc="vec")
    r.wait("sync", "vec")
    dma("sync", cnt_in, MISC[0:1, M_CNT:M_CNT + 16], inc="spill")
    r.wait("gpsimd", "spill")
    r.op("gpsimd", "collective_compute", "AllGather", ALU.bypass, RG,
         [cnt_in], [cnt_all], inc="coll")
    r.wait("sync", "coll", 2)
    gld = dma("sync", MISC[0:8, M_CNTA:M_CNTA + 16], cnt_all)
    r.wait("tensor", "ld", gld)
    r.wait("tensor", "vec")
    pb = PSB[6]
    mm(pb[0:1, 0:16], CST[0:8, C_PM:C_PM + 1], MISC[0:8, M_CNTA:M_CNTA + 16],
       start=True, stop=True, inc="mm")
    r.wait("vector", "mm")
    vop("tensor_copy", SMALL[0:1, 0:16], pb[0:1, 0:16], inc="vec")
    r.wait("tensor", "vec")
    mm(pb[:, 16:32], CST[0:1, C_ON:C_ON + 128], SMALL[0:1, 0:16], start=True, stop=True,
       inc="mm")
    r.wait("vector", "mm")
    vop("tensor_copy", MISC[:, M_BASE:M_BASE + 16], pb[:, 16:32], inc="vec")
    for tt in range(8):
        vop("tensor_tensor", MISC[:, M_PG + tt * 16:M_PG + (tt + 1) * 16],
            MISC[:, M_P + tt * 16:M_P + (tt + 1) * 16],
            MISC[:, M_BASE:M_BASE + 16], ALU.add)
        qb = M_Q + tt * 4
        oh1 = MISC[:, M_OH1 + tt * 16:M_OH1 + (tt + 1) * 16]
        oh2 = MISC[:, M_OH2 + tt * 16:M_OH2 + (tt + 1) * 16]
        pg = MISC[:, M_PG + tt * 16:M_PG + (tt + 1) * 16]
        vop("tensor_tensor", SMALL[:, 0:16], pg, oh1, ALU.mult, inc="vec")
        r.wait("vector", "vec")
        vop("reduce_sum", SMALL[:, 16:17], SMALL[:, 0:16], AX.X, inc="vec")
        r.wait("vector", "vec")
        vop("tensor_tensor", SMALL[:, 0:16], pg, oh2, ALU.mult, inc="vec")
        r.wait("vector", "vec")
        vop("reduce_sum", SMALL[:, 17:18], SMALL[:, 0:16], AX.X, inc="vec")
        r.wait("vector", "vec")
        for rnk in range(2):
            gp_ = SMALL[:, 16 + rnk:17 + rnk]
            ecol = MISC[:, qb + rnk:qb + rnk + 1]
            vop("tensor_scalar", SMALL[:, 20 + rnk:21 + rnk], ecol, 1024.0, None, ALU.mult)
            vop("tensor_tensor", SMALL[:, 20 + rnk:21 + rnk], SMALL[:, 20 + rnk:21 + rnk],
                gp_, ALU.add)
            vop("tensor_scalar", SMALL[:, 24 + rnk:25 + rnk], gp_, 1024.0, None, ALU.is_lt)
            vop("tensor_tensor", MISC[:, qb + 2 + rnk:qb + 3 + rnk],
                MISC[:, qb + 2 + rnk:qb + 3 + rnk], SMALL[:, 24 + rnk:25 + rnk], ALU.mult)
            vop("tensor_copy", SMALL[:, 36 + rnk:37 + rnk].bitcast(I32),
                SMALL[:, 24 + rnk:25 + rnk], inc="vec")
            r.wait("vector", "vec")
            vop("select", SMALL[:, 28 + rnk:29 + rnk], SMALL[:, 36 + rnk:37 + rnk].bitcast(I32),
                SMALL[:, 20 + rnk:21 + rnk], CST[:, C_99:C_99 + 1])
            vop("tensor_copy", SMALL[:, 32 + rnk:33 + rnk].bitcast(I32),
                SMALL[:, 28 + rnk:29 + rnk])
            vop("tensor_copy", MISC[:, qb + rnk:qb + rnk + 1].bitcast(I32),
                SMALL[:, 20 + rnk:21 + rnk], inc="vec")
        r.wait("gpsimd", "vec")
        r.wait("gpsimd", "spill", 16)
        for rnk in range(2):
            r.op("gpsimd", "indirect_dma_start", glist_in,
                 bass.IndirectOffsetOnAxis(SMALL[:, 32 + rnk:33 + rnk].bitcast(I32), 0),
                 CST[:, C_TOK + tt:C_TOK + tt + 1], None,
                 bounds_check=("REG", E * CAP - 1), oob_is_err=False, inc="gp", incv=16)
    r.wait("gpsimd", "gp")
    r.op("gpsimd", "collective_compute", "ReduceScatter", ALU.add, RG,
         [glist_in], [glist_own], inc="coll")

    def inproj_ftile_range(frange):
        r.wait("sync", "spill")
        for tc in range(4):
            r.wait("sync", "mm")
            hld = dma("sync", CL1[:, 0:4096].bitcast(F32R),
                      fm_ap(xnt_hi_d, 128, 16, 256, 0, tc * 256, TH_P))
            lld = dma("sync", CL2[:, 0:4096].bitcast(F32R),
                      fm_ap(xnt_lo_d, 128, 16, 256, 0, tc * 256, TH_P))
            split_done = {0: 0, 1: 0}
            for fi, f in enumerate(frange):
                fw = 64 if f == 66 else 128
                raw = 0
                r.wait("sync", "vec")
                ldv = dma("sync", WS[:, raw:raw + 16 * fw],
                          fm_ap(w_in, 128, 16, fw, 0, f * 128, DPROJ))
                r.wait("vector", "ld", ldv)
                r.wait("vector", "mm")
                for k in range(16):
                    hilo(WH[:, k * fw:(k + 1) * fw], WL[:, k * fw:(k + 1) * fw],
                         WS[:, raw + k * fw:raw + (k + 1) * fw])
                mark()
                split_done[fi % 2] = r.cnt["vec"]
                r.wait("tensor", "vec")
                if fi == 0:
                    r.wait("tensor", "ld", lld)
                pb = PSB[fi % 2]
                for k in range(16):
                    wh = WH[:, k * fw:(k + 1) * fw].bitcast(F32R)
                    wl = WL[:, k * fw:(k + 1) * fw].bitcast(F32R)
                    xh = CL1[:, k * 256:(k + 1) * 256].bitcast(F32R)
                    xl = CL2[:, k * 256:(k + 1) * 256].bitcast(F32R)
                    mm(pb[0:fw, 0:256], wh, xh, start=(k == 0), stop=False)
                    mm(pb[0:fw, 0:256], wh, xl, start=False, stop=False)
                    mm(pb[0:fw, 0:256], wl, xh, start=False, stop=(k == 15),
                       inc="mm" if k == 15 else None)
                r.wait("vector", "mm")
                r.wait("vector", "spill")
                eb = (fi % 2) * 256
                vop("tensor_copy", SMALL[0:fw, eb:eb + 256], pb[0:fw, 0:256], inc="vec")
                r.wait("sync", "vec")
                dma("sync", bass.AP(zx_d.tensor, f * 128 * TH + tc * 256, [[TH, fw], [1, 256]]),
                    SMALL[0:fw, eb:eb + 256], inc="spill")

    def inproj_halo():
        r.wait("sync", "mm")
        hh = dma("sync", CL1[:, 4096:4352].bitcast(F32R),
                 fm_ap(xnt_hi_d, 128, 16, 16, 0, 1024, TH_P))
        hl = dma("sync", CL2[:, 4096:4352].bitcast(F32R),
                 fm_ap(xnt_lo_d, 128, 16, 16, 0, 1024, TH_P))
        split_done = {0: 0, 1: 0}
        for fi, f in enumerate(range(32, 66)):
            raw = 0
            r.wait("sync", "vec")
            ldv = dma("sync", WS[:, raw:raw + 2048],
                      fm_ap(w_in, 128, 16, 128, 0, f * 128, DPROJ))
            r.wait("vector", "ld", ldv)
            r.wait("vector", "mm")
            for k in range(16):
                hilo(WH[:, k * 128:(k + 1) * 128], WL[:, k * 128:(k + 1) * 128],
                     WS[:, raw + k * 128:raw + (k + 1) * 128])
            mark()
            split_done[fi % 2] = r.cnt["vec"]
            r.wait("tensor", "vec")
            if fi == 0:
                r.wait("tensor", "ld", hl)
            pb = PSB[6]
            for k in range(16):
                wh = WH[:, k * 128:(k + 1) * 128].bitcast(F32R)
                wl = WL[:, k * 128:(k + 1) * 128].bitcast(F32R)
                xh = CL1[:, 4096 + k * 16:4096 + (k + 1) * 16].bitcast(F32R)
                xl = CL2[:, 4096 + k * 16:4096 + (k + 1) * 16].bitcast(F32R)
                mm(pb[:, 0:16], wh, xh, start=(k == 0), stop=False)
                mm(pb[:, 0:16], wh, xl, start=False, stop=False)
                mm(pb[:, 0:16], wl, xh, start=False, stop=(k == 15),
                   inc="mm" if k == 15 else None)
            r.wait("vector", "mm")
            r.wait("vector", "spill")
            eb = 768 + (fi % 2) * 3
            vop("tensor_copy", SMALL[:, eb:eb + 3], pb[:, 0:3], inc="vec")
            r.wait("sync", "vec")
            dma("sync", bass.AP(zx_d.tensor, f * 128 * TH + 1024, [[TH, 128], [1, 3]]),
                SMALL[:, eb:eb + 3], inc="spill")

    if phases < 5:
        return finish()
    inproj_ftile_range(list(range(32, 67)))
    inproj_halo()

    def chunk_prep(c, do_states, s_dst, do_cuma_spill):
        r.wait("sync", "spill")
        r.wait("vector", "mm")
        for ct in range(34):
            cb = (ct % 2) * 259
            r.wait("sync", "vec")
            if c == 0:
                dma("sync", CHIN[:, cb:cb + 3],
                    bass.AP(zx_d.tensor, (4096 + ct * 128) * TH + 1024, [[TH, 128], [1, 3]]))
                d2 = dma("sync", CHIN[:, cb + 3:cb + 259],
                         bass.AP(zx_d.tensor, (4096 + ct * 128) * TH, [[TH, 128], [1, 256]]))
            else:
                d2 = dma("sync", CHIN[:, cb:cb + 259],
                         bass.AP(zx_d.tensor, (4096 + ct * 128) * TH + c * 256 - 3,
                                 [[TH, 128], [1, 259]]))
            r.wait("vector", "ld", d2)
            r.wait("vector", "sca")
            ob = (ct % 2) * 256
            acc = CONVO[:, ob:ob + 256]
            vop("tensor_scalar", acc, CHIN[:, cb:cb + 256],
                CST[:, C_CW + ct * 4:C_CW + ct * 4 + 1], None, ALU.mult)
            for kk in range(1, 4):
                vop("scalar_tensor_tensor", acc, CHIN[:, cb + kk:cb + kk + 256],
                    CST[:, C_CW + ct * 4 + kk:C_CW + ct * 4 + kk + 1], acc, ALU.mult, ALU.add)
            mark()
            r.wait("scalar", "vec")
            r.wait("scalar", "mm")
            act(acc, acc, AF.Silu, bias=CST[:, C_CB + ct:C_CB + ct + 1], inc="sca")
            if ct < 32:
                r.wait("tensor", "sca")
                for lt in range(2):
                    pb = PSB[2 + lt]
                    r.wait("tensor", "vec")
                    tpose(pb[0:128, 0:128], acc[:, lt * 128:(lt + 1) * 128], inc="mm")
                    r.wait("vector", "mm")
                    vop("tensor_copy", D1[:, lt * 4096 + ct * 128:lt * 4096 + (ct + 1) * 128],
                        pb[0:128, 0:128], inc="vec")
            elif ct == 32:
                r.wait("vector", "sca")
                vop("tensor_copy", BCB[:, 0:256], acc, inc="vec")
                r.wait("tensor", "sca")
                for st in range(2):
                    pb = PSB[2 + st]
                    r.wait("tensor", "vec")
                    tpose(pb[0:128, 0:128], acc[:, st * 128:(st + 1) * 128], inc="mm")
                    r.wait("vector", "mm")
                    vop("tensor_copy", BCB[:, 512 + st * 128:512 + (st + 1) * 128],
                        pb[0:128, 0:128], inc="vec")
            else:
                r.wait("vector", "sca")
                vop("tensor_copy", BCB[:, 256:512], acc, inc="vec")
        r.wait("sync", "mm")
        r.wait("sync", "vec")
        dd = dma("sync", SMALL[0:64, 0:256],
                 bass.AP(zx_d.tensor, 8448 * TH + c * 256, [[TH, 64], [1, 256]]))
        r.wait("scalar", "ld", dd)
        act(SMALL[0:64, 256:512], SMALL[0:64, 0:256], AF.Exp,
            bias=CST[0:64, C_DTB:C_DTB + 1], inc="sca")
        r.op("scalar", "add", SMALL[0:64, 256:512], SMALL[0:64, 256:512], 1.0)
        act(SMALL[0:64, 256:512], SMALL[0:64, 256:512], AF.Ln, inc="sca")
        r.wait("vector", "sca")
        vop("tensor_scalar", SMALL[0:64, 512:768], SMALL[0:64, 256:512],
            CST[0:64, C_A:C_A + 1], None, ALU.mult, inc="vec")
        r.wait("vector", "vec")
        vop("reduce_sum", MISC[0:64, M_AL:M_AL + 1], SMALL[0:64, 512:768], AX.X, inc="vec")
        r.wait("vector", "vec")
        r.wait("scalar", "vec")
        act(MISC[0:64, M_AL + 1:M_AL + 2], MISC[0:64, M_AL:M_AL + 1], AF.Exp, inc="sca")
        r.wait("tensor", "vec")
        for st in range(2):
            pb = PSB[2 + st]
            tpose(pb[0:128, 0:64], SMALL[0:64, 256 + st * 128:256 + (st + 1) * 128], inc="mm")
            r.wait("vector", "mm")
            vop("tensor_copy", MISC[:, M_DT + st * 64:M_DT + (st + 1) * 64], pb[0:128, 0:64],
                inc="vec")
            r.wait("tensor", "vec")
            tpose(pb[0:128, 0:64], SMALL[0:64, 512 + st * 128:512 + (st + 1) * 128], inc="mm")
            r.wait("vector", "mm")
            vop("tensor_copy", SMALL[:, 768 + st * 64:768 + (st + 1) * 64], pb[0:128, 0:64],
                inc="vec")
        for lt in range(2):
            pb = PSB[6]
            r.wait("tensor", "vec")
            for st in range(lt + 1):
                lhsT = CST[:, C_TI:C_TI + 128] if st == lt else CST[:, C_ON:C_ON + 128]
                mm(pb[:, 0:64], lhsT, SMALL[:, 768 + st * 64:768 + (st + 1) * 64],
                   start=(st == 0), stop=(st == lt), inc="mm" if st == lt else None)
            r.wait("vector", "mm")
            vop("tensor_copy", MISC[:, M_CUMA + lt * 64:M_CUMA + (lt + 1) * 64], pb[:, 0:64],
                inc="vec")
        r.wait("scalar", "vec")
        act(MISC[:, M_EXPC:M_EXPC + 128], MISC[:, M_CUMA:M_CUMA + 128], AF.Exp, inc="sca")
        if do_cuma_spill:
            r.wait("tensor", "vec")
            for lt in range(2):
                pb = PSB[2 + lt]
                tpose(pb[0:64, 0:128], MISC[0:128, M_CUMA + lt * 64:M_CUMA + (lt + 1) * 64],
                      inc="mm")
                r.wait("vector", "mm")
                vop("tensor_copy", SMALL[0:64, lt * 128:(lt + 1) * 128], pb[0:64, 0:128],
                    inc="vec")
            r.wait("sync", "vec")
            dma("sync", bass.AP(cumA_d.tensor, c * 256, [[T, 64], [1, 256]]),
                SMALL[0:64, 0:256], inc="spill")
        r.wait("tensor", "sca")
        pb = PSB[6]
        tpose(pb[0:1, 0:64], MISC[0:64, M_AL:M_AL + 1], inc="mm")
        r.wait("vector", "mm")
        vop("tensor_copy", SMALL[0:1, 384:448], pb[0:1, 0:64], inc="vec")
        r.wait("tensor", "vec")
        mm(pb[:, 64:128], CST[0:1, C_ON:C_ON + 128], SMALL[0:1, 384:448], start=True,
           stop=True, inc="mm")
        r.wait("vector", "mm")
        vop("tensor_copy", SMALL[:, 448:512], pb[:, 64:128], inc="vec")
        for lt in range(2):
            vop("tensor_tensor", MISC[:, M_DEC + lt * 64:M_DEC + (lt + 1) * 64],
                SMALL[:, 448:512], MISC[:, M_CUMA + lt * 64:M_CUMA + (lt + 1) * 64],
                ALU.subtract, inc="vec")
        r.wait("scalar", "vec")
        act(MISC[:, M_DEC:M_DEC + 128], MISC[:, M_DEC:M_DEC + 128], AF.Exp, inc="sca")
        r.wait("tensor", "sca")
        pb = PSB[6]
        tpose(pb[0:1, 0:64], MISC[0:64, M_AL + 1:M_AL + 2], inc="mm")
        r.wait("vector", "mm")
        vop("tensor_copy", SMALL[0:1, 384:448], pb[0:1, 0:64], inc="vec")
        r.wait("tensor", "vec")
        mm(pb[:, 64:128], CST[0:1, C_ON:C_ON + 128], SMALL[0:1, 384:448], start=True,
           stop=True, inc="mm")
        r.wait("vector", "mm")
        vop("tensor_copy", MISC[:, M_AL + 4:M_AL + 68], pb[:, 64:128], inc="vec")
        r.wait("vector", "vec")
        if do_states:
            r.wait("vector", "spill")
            r.wait("vector", "sca")
            for hg in range(8):
                pb = PSB[7]
                r.wait("tensor", "vec")
                for hh in range(8):
                    h = hg * 8 + hh
                    for st in range(2):
                        vop("tensor_scalar", SMALL[:, 0:128],
                            BCB[:, 512 + st * 128:512 + (st + 1) * 128],
                            MISC[:, M_DEC + st * 64 + h:M_DEC + st * 64 + h + 1], None,
                            ALU.mult)
                        vop("tensor_scalar", SMALL[:, 128:192],
                            D1[:, st * 4096 + h * 64:st * 4096 + (h + 1) * 64],
                            MISC[:, M_DT + st * 64 + h:M_DT + st * 64 + h + 1], None,
                            ALU.mult, inc="vec")
                        r.wait("tensor", "vec")
                        mm(pb[:, hh * 64:(hh + 1) * 64], SMALL[:, 0:128], SMALL[:, 128:192],
                           start=(st == 0), stop=(st == 1), inc="mm" if st == 1 else None)
                r.wait("vector", "mm")
                vop("tensor_copy", s_dst[:, hg * 512:(hg + 1) * 512], pb[:, 0:512], inc="vec")


    if phases < 6:
        return finish()
    for c in range(NCH):
        chunk_prep(c, do_states=True, s_dst=D2[:, 0:4096], do_cuma_spill=True)
        if c == 0:
            vop("tensor_copy", RD[:, 0:4096], D2[:, 0:4096])
            vop("tensor_copy", MISC[0:64, M_AL + 2:M_AL + 3], MISC[0:64, M_AL:M_AL + 1],
                inc="vec")
        else:
            for h in range(NH):
                vop("scalar_tensor_tensor", RD[:, h * 64:(h + 1) * 64],
                    RD[:, h * 64:(h + 1) * 64],
                    MISC[:, M_AL + 4 + h:M_AL + 5 + h], D2[:, h * 64:(h + 1) * 64],
                    ALU.mult, ALU.add)
            vop("tensor_tensor", MISC[0:64, M_AL + 2:M_AL + 3], MISC[0:64, M_AL + 2:M_AL + 3],
                MISC[0:64, M_AL:M_AL + 1], ALU.add, inc="vec")
    r.wait("sync", "vec")
    dma("sync", stat_in, RD[:, 0:4096], inc="spill")
    r.wait("tensor", "vec")
    pb = PSB[6]
    tpose(pb[0:1, 0:64], MISC[0:64, M_AL + 2:M_AL + 3], inc="mm")
    r.wait("vector", "mm")
    vop("tensor_copy", SMALL[0:1, 384:448], pb[0:1, 0:64], inc="vec")
    r.wait("sync", "vec")
    dma("sync", alast_in, SMALL[0:1, 384:448], inc="spill")
    r.wait("gpsimd", "spill")
    r.op("gpsimd", "collective_compute", "AllGather", ALU.bypass, RG,
         [stat_in], [stat_all], inc="coll")
    r.op("gpsimd", "collective_compute", "AllGather", ALU.bypass, RG,
         [alast_in], [alast_all], inc="coll")

    def ffn_run(xh_off, xl_off, w1src, w3src, w2src, dst_dram, dst_row0):
        for it in range(11):
            for mat, wsrc, pidx in ((0, w1src, 0), (1, w3src, 1)):
                r.wait("sync", "vec")
                ldv = dma("sync", WS[:, 0:2048], fm_ap(wsrc, 128, 16, 128, 0, it * 128, I_FF))
                r.wait("vector", "ld", ldv)
                r.wait("vector", "mm")
                for k in range(16):
                    hilo(WH[:, k * 128:(k + 1) * 128], WL[:, k * 128:(k + 1) * 128],
                         WS[:, k * 128:(k + 1) * 128])
                mark()
                r.wait("tensor", "vec")
                pb = PSB[pidx]
                for k in range(16):
                    wh = WH[:, k * 128:(k + 1) * 128].bitcast(F32R)
                    wl = WL[:, k * 128:(k + 1) * 128].bitcast(F32R)
                    xh = CL1[:, xh_off + k * 256:xh_off + (k + 1) * 256].bitcast(F32R)
                    xl = CL2[:, xl_off + k * 256:xl_off + (k + 1) * 256].bitcast(F32R)
                    mm(pb[:, 0:256], wh, xh, start=(k == 0), stop=False)
                    mm(pb[:, 0:256], wh, xl, start=False, stop=False)
                    mm(pb[:, 0:256], wl, xh, start=False, stop=(k == 15),
                       inc="mm" if k == 15 else None)
            r.wait("scalar", "mm")
            act(SMALL[:, 0:256], PSB[0][:, 0:256], AF.Silu, inc="sca")
            r.wait("vector", "sca")
            vop("tensor_tensor", SMALL[:, 256:512], SMALL[:, 0:256], PSB[1][:, 0:256],
                ALU.mult)
            hilo(CL1[:, 4096 + it * 256:4096 + (it + 1) * 256],
                 CL2[:, 4096 + it * 256:4096 + (it + 1) * 256],
                 SMALL[:, 256:512])
            mark()
        for dt_ in range(16):
            r.wait("sync", "vec")
            ldv = dma("sync", WS[:, 0:1408], fm_ap(w2src, 128, 11, 128, 0, dt_ * 128, D))
            r.wait("vector", "ld", ldv)
            r.wait("vector", "mm")
            for k in range(11):
                hilo(WH[:, k * 128:(k + 1) * 128], WL[:, k * 128:(k + 1) * 128],
                     WS[:, k * 128:(k + 1) * 128])
            mark()
            r.wait("tensor", "vec")
            pb = PSB[4]
            for k in range(11):
                wh = WH[:, k * 128:(k + 1) * 128].bitcast(F32R)
                wl = WL[:, k * 128:(k + 1) * 128].bitcast(F32R)
                hh_ = CL1[:, 4096 + k * 256:4096 + (k + 1) * 256].bitcast(F32R)
                hl_ = CL2[:, 4096 + k * 256:4096 + (k + 1) * 256].bitcast(F32R)
                mm(pb[:, 0:256], wh, hh_, start=(k == 0), stop=False)
                mm(pb[:, 0:256], wh, hl_, start=False, stop=False)
                mm(pb[:, 0:256], wl, hh_, start=False, stop=(k == 10),
                   inc="mm" if k == 10 else None)
            r.wait("vector", "mm")
            vop("tensor_copy", SMALL[:, 256:512], pb[:, 0:256], inc="vec")
            r.wait("tensor", "vec")
            for sub in range(2):
                pt = PSB[2 + sub]
                tpose(pt[0:128, 0:128], SMALL[:, 256 + sub * 128:256 + (sub + 1) * 128],
                      inc="mm")
                r.wait("vector", "mm")
                if dt_ == 0:
                    r.wait("vector", "spill")
                vop("tensor_copy",
                    D1[:, sub * 2048 + dt_ * 128:sub * 2048 + (dt_ + 1) * 128],
                    pt[0:128, 0:128], inc="vec")
        r.wait("sync", "vec")
        for sub in range(2):
            dma("sync", dst_dram[dst_row0 + sub * 128:dst_row0 + (sub + 1) * 128, :],
                D1[:, sub * 2048:(sub + 1) * 2048], inc="spill")

    if phases < 7:
        return finish()

    def hard_join():
        for e2 in ("sync", "tensor", "vector", "scalar", "gpsimd"):
            for s2 in ("ld", "mm", "vec", "sca", "gp", "spill"):
                r.wait(e2, s2)

    hard_join()
    r.wait("gpsimd", "coll", 3)
    r.wait("sync", "coll", 3)
    for le in range(2):
        w1e = bass.AP(w1T.tensor, le * D * I_FF, [[I_FF, D], [1, I_FF]])
        w3e = bass.AP(w3T.tensor, le * D * I_FF, [[I_FF, D], [1, I_FF]])
        w2e = bass.AP(w2T.tensor, le * I_FF * D, [[D, I_FF], [1, D]])
        for blk in range(4):
            for sub in range(2):
                slot0 = le * CAP + blk * 256 + sub * 128
                gl = dma("sync", SMALL[:, 640 + sub:641 + sub],
                         bass.AP(glist_own.tensor, slot0, [[1, 128], [1, 1]]))
                r.wait("vector", "ld", gl)
                r.wait("vector", "gp")
                vop("tensor_scalar", SMALL[:, 644 + sub:645 + sub],
                    SMALL[:, 640 + sub:641 + sub], 0.5, None, ALU.is_gt)
                vop("tensor_scalar", SMALL[:, 648 + sub:649 + sub],
                    SMALL[:, 640 + sub:641 + sub], 1.0, None, ALU.subtract)
                vop("tensor_copy", SMALL[:, 660 + sub:661 + sub].bitcast(I32),
                    SMALL[:, 644 + sub:645 + sub], inc="vec")
                r.wait("vector", "vec")
                vop("select", SMALL[:, 652 + sub:653 + sub],
                    SMALL[:, 660 + sub:661 + sub].bitcast(I32),
                    SMALL[:, 648 + sub:649 + sub], CST[:, C_99:C_99 + 1])
                vop("tensor_copy", SMALL[:, 656 + sub:657 + sub].bitcast(I32),
                    SMALL[:, 652 + sub:653 + sub])
                r.wait("vector", "mm")
                r.wait("vector", "spill")
                vop("memset", D1[:, sub * 2048:(sub + 1) * 2048], 0.0, inc="vec")
                r.wait("gpsimd", "vec")
                r.op("gpsimd", "indirect_dma_start", D1[:, sub * 2048:(sub + 1) * 2048], None,
                     xn_all,
                     bass.IndirectOffsetOnAxis(SMALL[:, 656 + sub:657 + sub].bitcast(I32), 0),
                     bounds_check=("REG", S_LEN - 1), oob_is_err=False, inc="gp", incv=16)
            r.wait("tensor", "gp")
            for sub in range(2):
                for k in range(16):
                    pt = PSB[2 + (k % 2)]
                    r.wait("tensor", "vec")
                    tpose(pt[0:128, 0:128],
                          D1[:, sub * 2048 + k * 128:sub * 2048 + (k + 1) * 128], inc="mm")
                    r.wait("vector", "mm")
                    hilo(CL1[:, k * 256 + sub * 128:k * 256 + (sub + 1) * 128],
                         CL2[:, k * 256 + sub * 128:k * 256 + (sub + 1) * 128],
                         pt[0:128, 0:128])
                    mark()
            ffn_run(0, 0, w1e, w3e, w2e, out_tab, le * CAP + blk * 256)
    hard_join()
    r.wait("gpsimd", "spill")
    r.op("gpsimd", "collective_compute", "AllGather", ALU.bypass, RG,
         [out_tab], [outs_all], inc="coll")

    if phases < 8:
        return finish()
    inproj_ftile_range(list(range(0, 32)))

    if phases < 9:
        return finish()
    r.wait("sync", "coll", 5)
    ga = dma("sync", SMALL[0:8, 384:448], alast_all)
    r.wait("tensor", "ld", ga)
    r.wait("tensor", "vec")
    pb = PSB[6]
    mm(pb[0:64, 0:8], SMALL[0:8, 384:448], CST[0:8, C_WMM:C_WMM + 8], start=True, stop=True,
       inc="mm")
    r.wait("vector", "mm")
    vop("tensor_tensor", SMALL[0:64, 448:456], pb[0:64, 0:8], CST[0:64, C_WMB:C_WMB + 8],
        ALU.add, inc="vec")
    r.wait("scalar", "vec")
    act(SMALL[0:64, 448:456], SMALL[0:64, 448:456], AF.Exp, inc="sca")
    r.wait("tensor", "sca")
    tpose(pb[0:8, 0:64], SMALL[0:64, 448:456], inc="mm")
    r.wait("vector", "mm")
    vop("tensor_copy", SMALL[0:8, 456:520], pb[0:8, 0:64])
    vop("memset", RC[:, 0:4096], 0.0, inc="vec")
    r.wait("sync", "vec")
    r.wait("sync", "spill")
    for j in range(8):
        sld = dma("sync", RD[:, 0:4096],
                  bass.AP(stat_all.tensor, j * 128 * 4096, [[4096, 128], [1, 4096]]))
        wjd = dma("sync", SMALL[0:1, 520:584], SMALL[j:j + 1, 456:520])
        r.wait("tensor", "ld", wjd)
        pb = PSB[6]
        r.wait("tensor", "vec")
        mm(pb[:, 64:128], CST[0:1, C_ON:C_ON + 128], SMALL[0:1, 520:584], start=True,
           stop=True, inc="mm")
        r.wait("vector", "mm")
        r.wait("vector", "ld", wjd)
        vop("tensor_copy", SMALL[:, 0:64], pb[:, 64:128], inc="vec")
        r.wait("vector", "vec")
        for h in range(NH):
            vop("scalar_tensor_tensor", RC[:, h * 64:(h + 1) * 64], RD[:, h * 64:(h + 1) * 64],
                SMALL[:, h:h + 1], RC[:, h * 64:(h + 1) * 64], ALU.mult, ALU.add)
        mark()
        r.wait("sync", "vec")

    if phases < 9.1:
        return finish()
    for c in range(NCH):
        if phases < 9.15 and c > 0:
            break
        if c > 0:
            for h in range(NH):
                vop("scalar_tensor_tensor", RC[:, h * 64:(h + 1) * 64],
                    RC[:, h * 64:(h + 1) * 64],
                    MISC[:, M_AL + 68 + h:M_AL + 69 + h], RD[:, h * 64:(h + 1) * 64],
                    ALU.mult, ALU.add)
            mark()
        chunk_prep(c, do_states=(c < 3), s_dst=RD[:, 0:4096], do_cuma_spill=False)
        vop("tensor_copy", MISC[:, M_AL + 68:M_AL + 132], MISC[:, M_AL + 4:M_AL + 68],
            inc="vec")
        if phases < 9.17:
            continue
        for st in range(2):
            pb = PSB[6]
            r.wait("tensor", "vec")
            mm(pb[:, 0:256], BCB[:, 0:256][:, st * 128:(st + 1) * 128], BCB[:, 256:512],
               start=True, stop=True, inc="mm")
            r.wait("vector", "mm")
            gm = CST[:, C_G0:C_G0 + 256] if st == 0 else CST[:, C_G1:C_G1 + 256]
            vop("tensor_tensor", GT[:, st * 256:(st + 1) * 256], pb[:, 0:256], gm, ALU.mult,
                inc="vec")
        if phases < 9.2:
            continue
        for hg in range(16):
            r.wait("sync", "vec")
            cl = dma("sync", RE[:, 0:1024],
                     bass.AP(cumA_d.tensor, hg * 4 * T + c * 256,
                             [[0, 128], [T, 4], [1, 256]]))
            r.wait("vector", "ld", cl)
            pbY0, pbO0 = PSB[4], PSB[5]
            pbY1, pbO1 = PSB[6], PSB[7]
            r.wait("tensor", "vec")
            for hh in range(4):
                h = hg * 4 + hh
                ca = RE[:, hh * 256:(hh + 1) * 256]
                for st in range(2):
                    vop("tensor_scalar", SMALL[:, 0:256], ca,
                        MISC[:, M_CUMA + st * 64 + h:M_CUMA + st * 64 + h + 1], 0.0,
                        ALU.subtract, ALU.min, inc="vec")
                    r.wait("scalar", "vec")
                    act(SMALL[:, 256:512], SMALL[:, 0:256], AF.Exp, inc="sca")
                    r.wait("vector", "sca")
                    vop("tensor_tensor", MT[:, st * 256:(st + 1) * 256],
                        GT[:, st * 256:(st + 1) * 256], SMALL[:, 256:512], ALU.mult)
                    vop("tensor_scalar", SMALL[:, 512 + st * 64:512 + (st + 1) * 64],
                        D1[:, st * 4096 + h * 64:st * 4096 + (h + 1) * 64],
                        MISC[:, M_DT + st * 64 + h:M_DT + st * 64 + h + 1], None, ALU.mult,
                        inc="vec")
                r.wait("tensor", "vec")
                for lt in range(2):
                    tY = (pbY0 if lt == 0 else pbY1)[:, hh * 64:(hh + 1) * 64]
                    tO = (pbO0 if lt == 0 else pbO1)[:, hh * 64:(hh + 1) * 64]
                    mm(tY, MT[:, lt * 128:lt * 128 + 128], SMALL[:, 512:576],
                       start=True, stop=False)
                    mm(tY, MT[:, 256 + lt * 128:256 + lt * 128 + 128], SMALL[:, 576:640],
                       start=False, stop=True, inc="mm")
                    mm(tO, BCB[:, 256:512][:, lt * 128:(lt + 1) * 128],
                       RC[:, h * 64:(h + 1) * 64], start=True, stop=True, inc="mm")
            r.wait("vector", "mm")
            for lt in range(2):
                pY = pbY0 if lt == 0 else pbY1
                pO = pbO0 if lt == 0 else pbO1
                vop("tensor_copy", SMALL[:, 256:512], pY[:, 0:256])
                for hh in range(4):
                    h = hg * 4 + hh
                    ycol = D2[:, lt * 4096 + h * 64:lt * 4096 + (h + 1) * 64]
                    vop("scalar_tensor_tensor", ycol, pO[:, hh * 64:(hh + 1) * 64],
                        MISC[:, M_EXPC + lt * 64 + h:M_EXPC + lt * 64 + h + 1],
                        SMALL[:, 256 + hh * 64:256 + (hh + 1) * 64], ALU.mult, ALU.add)
                    vop("scalar_tensor_tensor", ycol,
                        D1[:, lt * 4096 + h * 64:lt * 4096 + (h + 1) * 64],
                        CST[:, C_DCOL + h:C_DCOL + h + 1], ycol, ALU.mult, ALU.add)
            mark()
        if phases < 9.3:
            continue
        for lt in range(2):
            yl = D2[:, lt * 4096:(lt + 1) * 4096]
            for ct in range(32):
                r.wait("sync", "mm")
                zl = dma("sync", SMALL[:, 0:128],
                         bass.AP(zx_d.tensor, (ct * 128) * TH + c * 256 + lt * 128,
                                 [[TH, 128], [1, 128]]))
                r.wait("tensor", "ld", zl)
                r.wait("tensor", "sca")
                pt = PSB[2 + (ct % 2)]
                tpose(pt[0:128, 0:128], SMALL[:, 0:128], inc="mm")
                r.wait("scalar", "mm")
                r.wait("scalar", "vec")
                act(SMALL[:, 128:256], pt[0:128, 0:128], AF.Silu, inc="sca")
                r.wait("vector", "sca")
                vop("tensor_tensor", yl[:, ct * 128:(ct + 1) * 128],
                    yl[:, ct * 128:(ct + 1) * 128], SMALL[:, 128:256], ALU.mult, inc="vec")
                r.wait("sync", "vec")
            r.wait("scalar", "vec")
            for seg in range(4):
                act(RE[:, 0:1024],
                    yl[:, seg * 1024:(seg + 1) * 1024], AF.Square, inc="sca")
                r.wait("vector", "sca")
                vop("reduce_sum", SMALL[:, 4 + seg:5 + seg], RE[:, 0:1024], AX.X, inc="vec")
                r.wait("vector", "vec")
                r.wait("scalar", "vec")
            vop("tensor_tensor", SMALL[:, 0:1], SMALL[:, 4:5], SMALL[:, 5:6], ALU.add)
            vop("tensor_tensor", SMALL[:, 1:2], SMALL[:, 6:7], SMALL[:, 7:8], ALU.add)
            vop("tensor_tensor", SMALL[:, 0:1], SMALL[:, 0:1], SMALL[:, 1:2], ALU.add,
                inc="vec")
            r.wait("scalar", "vec")
            act(SMALL[:, 1:2], SMALL[:, 0:1], AF.Sqrt, bias=CST[:, C_EPS:C_EPS + 1],
                scale=1.0 / DIN, inc="sca")
            r.wait("vector", "sca")
            vop("reciprocal", SMALL[:, 2:3], SMALL[:, 1:2], inc="vec")
            r.wait("vector", "vec")
            vop("tensor_scalar", yl, yl, SMALL[:, 2:3], None, ALU.mult, inc="vec")
            for piece in range(4):
                r.wait("sync", "vec")
                nld = dma("sync", RE[:, 0:1024],
                          bass.AP(ns_row.tensor, piece * 1024, [[0, 128], [1, 1024]]))
                r.wait("vector", "ld", nld)
                vop("tensor_tensor", yl[:, piece * 1024:(piece + 1) * 1024],
                    yl[:, piece * 1024:(piece + 1) * 1024], RE[:, 0:1024], ALU.mult,
                    inc="vec")
            r.wait("tensor", "vec")
            for k in range(32):
                pt = PSB[2 + (k % 2)]
                if k >= 2:
                    r.wait("tensor", "vec")
                tpose(pt[0:128, 0:128], yl[:, k * 128:(k + 1) * 128], inc="mm")
                r.wait("vector", "mm")
                hilo(CL1[:, k * 128:(k + 1) * 128], CL2[:, k * 128:(k + 1) * 128],
                     pt[0:128, 0:128])
                mark()
            if phases < 9.4:
                continue
            prevw = {0: 0, 1: 0}
            for k in range(32):
                base = 0
                r.wait("sync", "vec")
                wld = dma("sync", WS[:, base:base + 2048],
                          bass.AP(w_out.tensor, k * 128 * D, [[D, 128], [1, 2048]]))
                r.wait("vector", "ld", wld)
                r.wait("vector", "mm")
                for kk in range(4):
                    hilo(WH[:, kk * 512:(kk + 1) * 512], WL[:, kk * 512:(kk + 1) * 512],
                         WS[:, base + kk * 512:base + (kk + 1) * 512])
                mark()
                r.wait("tensor", "vec")
                for dch in range(4):
                    pb = PSB[[0, 1, 4, 5][dch]]
                    yh = CL1[:, k * 128:(k + 1) * 128].bitcast(F32R)
                    ylo = CL2[:, k * 128:(k + 1) * 128].bitcast(F32R)
                    wh = WH[:, dch * 512:(dch + 1) * 512].bitcast(F32R)
                    wl = WL[:, dch * 512:(dch + 1) * 512].bitcast(F32R)
                    mm(pb[:, 0:512], yh, wh, start=(k == 0), stop=False)
                    mm(pb[:, 0:512], yh, wl, start=False, stop=False)
                    mm(pb[:, 0:512], ylo, wh, start=False, stop=(k == 31),
                       inc="mm" if (k == 31 and dch == 3) else None)
            r.wait("vector", "mm")
            for dch in range(4):
                pb = PSB[[0, 1, 4, 5][dch]]
                vop("tensor_copy", D1[:, 4096 + dch * 512:4096 + (dch + 1) * 512],
                    pb[:, 0:512], inc="vec")
            r.wait("sync", "vec")
            dma("sync", s_d[(c * 2 + lt) * 128:(c * 2 + lt + 1) * 128, :],
                D1[:, 4096:6144], inc="spill")

    if phases < 10:
        return finish()
    for blk in range(4):
        r.wait("sync", "mm")
        bh = dma("sync", CL1[:, 0:4096].bitcast(F32R),
                 fm_ap(xnt_hi_d, 128, 16, 256, 0, blk * 256, TH_P))
        bl = dma("sync", CL2[:, 0:4096].bitcast(F32R),
                 fm_ap(xnt_lo_d, 128, 16, 256, 0, blk * 256, TH_P))
        r.wait("tensor", "ld", bl)
        ffn_run(0, 0, ws1T, ws3T, ws2T, sh_d, blk * 256)

    if phases < 11:
        return finish()
    r.wait("gpsimd", "coll", 6)
    r.wait("sync", "spill")
    for tt in range(8):
        qb = M_Q + tt * 4
        r.wait("vector", "mm")
        r.wait("vector", "st")
        r.wait("vector", "spill")
        vop("memset", D1[:, 0:2048], 0.0)
        vop("memset", D1[:, 2048:4096], 0.0, inc="vec")
        r.wait("gpsimd", "vec")
        for rnk in range(2):
            r.op("gpsimd", "indirect_dma_start", D1[:, rnk * 2048:(rnk + 1) * 2048], None,
                 outs_all,
                 bass.IndirectOffsetOnAxis(MISC[:, qb + rnk:qb + rnk + 1].bitcast(I32), 0),
                 bounds_check=("REG", E * CAP - 1), oob_is_err=False, inc="gp", incv=16)
        r.wait("sync", "vec")
        uld = dma("sync", D1[:, 4096:6144], u[tt * 128:(tt + 1) * 128, :])
        sld = dma("sync", D1[:, 6144:8192], s_d[tt * 128:(tt + 1) * 128, :])
        shld = dma("sync", D2[:, 0:2048], sh_d[tt * 128:(tt + 1) * 128, :])
        r.wait("vector", "gp")
        r.wait("vector", "ld", shld)
        if DEBUG and tt == 0:
            dbg_g = nc.dram_tensor("dbg_gath", [256, 2048], F32, kind="ExternalOutput").ap()
            r.wait("sync", "gp")
            dma("sync", dbg_g[0:128], D1[:, 0:2048], inc="spill")
            dma("sync", dbg_g[128:256], D1[:, 2048:4096], inc="spill")
            r.wait("vector", "spill")
        vop("tensor_scalar", D1[:, 0:2048], D1[:, 0:2048], MISC[:, qb + 2:qb + 3], None,
            ALU.mult)
        vop("scalar_tensor_tensor", D1[:, 0:2048], D1[:, 2048:4096], MISC[:, qb + 3:qb + 4],
            D1[:, 0:2048], ALU.mult, ALU.add)
        vop("tensor_tensor", D1[:, 0:2048], D1[:, 0:2048], D2[:, 0:2048], ALU.add, inc="vec")
        r.wait("scalar", "vec")
        act(D1[:, 0:2048], D1[:, 0:2048], AF.Silu, inc="sca")
        r.wait("vector", "sca")
        vop("tensor_tensor", D1[:, 0:2048], D1[:, 0:2048], D1[:, 6144:8192], ALU.mult)
        vop("tensor_tensor", D1[:, 0:2048], D1[:, 0:2048], D1[:, 4096:6144], ALU.add,
            inc="vec")
        r.wait("sync", "vec")
        dma("sync", out[tt * 128:(tt + 1) * 128, :], D1[:, 0:2048], inc="st")
    if DEBUG:
        dbg_oall = nc.dram_tensor("dbg_oall", [2048, 2048], F32, kind="ExternalOutput").ap()
        for i in range(16):
            r.wait("sync", "st")
            dma("sync", D2[0:128, 0:2048], outs_all[i * 1024:i * 1024 + 128], inc="ld")
            r.wait("sync", "ld")
            dma("sync", dbg_oall[i * 128:(i + 1) * 128], D2[0:128, 0:2048], inc="st")
        dbg_misc = nc.dram_tensor("dbg_misc", [128, 1536], F32, kind="ExternalOutput").ap()
        dbg_cst = nc.dram_tensor("dbg_cst", [128, 1344], F32, kind="ExternalOutput").ap()
        r.wait("sync", "st")
        dma("sync", dbg_misc, MISC[:, 0:1536], inc="st")
        dma("sync", dbg_cst, CST[:, 0:1344], inc="st")
        for nm, srcd, nrows in [("dbg_xn", xn_tm, 1024), ("dbg_outtab", out_tab, 2048),
                                ("dbg_glist", glist_own, 2048), ("dbg_cnt", cnt_all, 8)]:
            shp = list(srcd.tensor.shape)
            dd = nc.dram_tensor(nm, shp, F32, kind="ExternalOutput").ap()
            ncols = shp[1] if len(shp) > 1 else 1
            rows_per = max(1, min(128, 16384 // max(ncols, 1)))
            r0 = 0
            while r0 < nrows:
                nr = min(rows_per, nrows - r0)
                r.wait("sync", "st")
                dma("sync", D2[0:nr, 0:ncols], srcd[r0:r0 + nr], inc="ld")
                r.wait("sync", "ld")
                dma("sync", dd[r0:r0 + nr], D2[0:nr, 0:ncols], inc="st")
                r0 += nr
    r.wait("sync", "st")

    with nc.Block() as block:
        @block.sync
        def _(sync):
            r.replay(sync, "sync")

        @block.tensor
        def _(tensor):
            r.replay(tensor, "tensor")

        @block.vector
        def _(vector):
            r.replay(vector, "vector")

        @block.scalar
        def _(scalar):
            r.replay(scalar, "scalar")

        @block.gpsimd
        def _(gpsimd):
            r.replay(gpsimd, "gpsimd")

    stack.close()
    return nc


def _consts(core):
    c = {}
    c["ident"] = np.eye(128, dtype=np.float32)
    tri_s = np.zeros((128, 128), np.float32)
    tri_i = np.zeros((128, 128), np.float32)
    for a in range(128):
        tri_s[a, a + 1:] = 1.0
        tri_i[a, a:] = 1.0
    c["triu_strict"] = tri_s
    c["triu_inc"] = tri_i
    c["allones"] = np.ones((128, 128), np.float32)
    g0 = np.zeros((128, 256), np.float32)
    g1 = np.zeros((128, 256), np.float32)
    for sp in range(128):
        g0[sp, sp:] = 1.0
        g1[sp, 128 + sp:] = 1.0
    c["gmask0"] = g0
    c["gmask1"] = g1
    c["iota16"] = np.tile(np.arange(16, dtype=np.float32), (128, 1))
    pm = np.zeros((8, 1), np.float32)
    pm[:core, 0] = 1.0
    c["premask"] = pm
    wm = np.zeros((8, 8), np.float32)
    for k in range(8):
        for j in range(8):
            if j < k < core:
                wm[k, j] = 1.0
    c["wmaskmat"] = wm
    wb = np.zeros((64, 8), np.float32)
    wb[:, min(core, 8):] = -1e30
    if core == 0:
        wb[:, :] = -1e30
    c["wmaskbias"] = wb
    c["tokid1"] = (core * T + np.arange(T, dtype=np.float32) + 1.0).reshape(8, 128).T.copy()
    c["c99999"] = np.full((128, 1), 99999.0, np.float32)
    c["ceps"] = np.full((128, 1), EPS, np.float32)
    return c


def kernel(L0_attn_u, L0_attn_W_in, L0_attn_conv_w, L0_attn_conv_b, L0_attn_dt_bias,
           L0_attn_A_log, L0_attn_D, L0_attn_norm_scale, L0_attn_W_out,
           L0_ffn_input_ids, L0_ffn_gate_w, L0_ffn_gate_b, L0_ffn_W1, L0_ffn_W3,
           L0_ffn_W2, L0_ffn_Ws1, L0_ffn_Ws3, L0_ffn_Ws2):
    u_full = np.asarray(L0_attn_u, np.float32).reshape(S_LEN, D)
    w_in = np.ascontiguousarray(np.asarray(L0_attn_W_in, np.float32))
    w_out = np.ascontiguousarray(np.asarray(L0_attn_W_out, np.float32))
    cw = np.asarray(L0_attn_conv_w, np.float32)
    conv_wT = np.ascontiguousarray(cw.T.reshape(34, 128, 4).transpose(1, 0, 2).reshape(128, 136))
    conv_b = np.ascontiguousarray(np.asarray(L0_attn_conv_b, np.float32).reshape(34, 128).T)
    dt_bias = np.asarray(L0_attn_dt_bias, np.float32).reshape(NH, 1)
    a_log = np.asarray(L0_attn_A_log, np.float32).reshape(NH, 1)
    d_row = np.asarray(L0_attn_D, np.float32).reshape(1, NH)
    ns_row = np.asarray(L0_attn_norm_scale, np.float32).reshape(1, DIN)
    gate_wT = np.ascontiguousarray(np.asarray(L0_ffn_gate_w, np.float32).T)
    gate_b = np.asarray(L0_ffn_gate_b, np.float32).reshape(1, E)
    ws1T = np.ascontiguousarray(np.asarray(L0_ffn_Ws1, np.float32).T)
    ws3T = np.ascontiguousarray(np.asarray(L0_ffn_Ws3, np.float32).T)
    ws2T = np.ascontiguousarray(np.asarray(L0_ffn_Ws2, np.float32).T)
    w1T = np.ascontiguousarray(np.asarray(L0_ffn_W1, np.float32).transpose(0, 2, 1))
    w3T = np.ascontiguousarray(np.asarray(L0_ffn_W3, np.float32).transpose(0, 2, 1))
    w2T = np.ascontiguousarray(np.asarray(L0_ffn_W2, np.float32).transpose(0, 2, 1))

    nc = build_program()
    in_maps = []
    for core in range(NCORES):
        sh = np.zeros((TH, D), np.float32)
        sh[0:T] = u_full[core * T:(core + 1) * T]
        if core > 0:
            sh[T:TH] = u_full[core * T - 3:core * T]
        m = {
            "u": sh, "w_in": w_in, "w_out": w_out, "conv_wT": conv_wT, "conv_b": conv_b,
            "dt_bias": dt_bias, "a_log": a_log, "d_row": d_row, "ns_row": ns_row,
            "gate_wT": gate_wT, "gate_b": gate_b,
            "ws1T": ws1T, "ws3T": ws3T, "ws2T": ws2T,
            "w1T": w1T[2 * core:2 * core + 2], "w3T": w3T[2 * core:2 * core + 2],
            "w2T": w2T[2 * core:2 * core + 2],
        }
        m.update(_consts(core))
        in_maps.append(m)
    res = run_bass_kernel_spmd(nc, in_maps, core_ids=list(range(NCORES)))
    global LAST_RES
    if DEBUG:
        LAST_RES = res
    outp = np.concatenate([res.results[c]["out"] for c in range(NCORES)], axis=0)
    return outp.reshape(1, S_LEN, D)
